# version 06
# baseline (speedup 1.0000x reference)
import numpy as np
import ml_dtypes
import concourse.bass as bass
import concourse.mybir as mybir
from concourse.bass_utils import run_bass_kernel_spmd

F32 = mybir.dt.float32
BF16 = mybir.dt.bfloat16
AF = mybir.ActivationFunctionType
ALU = mybir.AluOpType
AX = mybir.AxisListType

P = 128
DC = 8
FC = 32
T = 2048
HT = 1024
NT = 16
NTH = 8
NE = 8
NSEQ = 2
NCORE = 8
TBL = 512
NTB = T // TBL


class B:
    ENGS = ("tensor", "vector", "scalar", "sync", "gpsimd")

    def __init__(self, nc):
        self.nc = nc
        self.plan = {k: [] for k in self.ENGS}
        self.sems = {}

    def sem(self, name):
        cm = self.nc.semaphore(name)
        h = cm.__enter__()
        self.sems[name] = [h, 0, cm]

    def close_sems(self):
        for name, (h, v, cm) in list(self.sems.items()):
            cm.__exit__(None, None, None)

    def v(self, name):
        return self.sems[name][1]

    def op(self, eng, fn, inc=None, n=1):
        if inc is not None:
            s = self.sems[inc]
            s[1] += n
            self.plan[eng].append(("o", fn, s[0], n))
            return s[1]
        self.plan[eng].append(("o", fn, None, 0))
        return None

    def wait(self, eng, sname, val):
        if val is None or val <= 0:
            return
        self.plan[eng].append(("w", self.sems[sname][0], val))

    def replay(self, eng, E):
        for it in self.plan[eng]:
            if it[0] == "w":
                E.wait_ge(it[1], it[2])
            else:
                r = it[1](E)
                if it[2] is not None:
                    r.then_inc(it[2], it[3])


def build_nc():
    nc = bass.Bass("TRN2", target_bir_lowering=False)

    x_in = nc.dram_tensor("x", [NSEQ * T, 1024], F32, kind="ExternalInput").ap()
    w1 = nc.dram_tensor("w1", [NE * 1024, 4096], F32, kind="ExternalInput").ap()
    w2 = nc.dram_tensor("w2", [NE * 4096, 1024], F32, kind="ExternalInput").ap()
    basisT_d = nc.dram_tensor("basisT", [256, 1024], F32, kind="ExternalInput").ap()
    coefT_d = [nc.dram_tensor(n, [256, 256], F32, kind="ExternalInput").ap()
               for n in ("qcT", "kcT", "vcT", "ocT")]
    wred_d = nc.dram_tensor("wred", [1024, 16], F32, kind="ExternalInput").ap()
    wg_d = nc.dram_tensor("wg", [8, 16], F32, kind="ExternalInput").ap()
    b1t_d = nc.dram_tensor("b1t", [128, 256], F32, kind="ExternalInput").ap()
    b2t_d = nc.dram_tensor("b2t", [128, 64], F32, kind="ExternalInput").ap()
    dlog_d = nc.dram_tensor("dlog", [128, 1], F32, kind="ExternalInput").ap()
    oscale_d = nc.dram_tensor("oscale", [128, 1], F32, kind="ExternalInput").ap()
    ident_d = nc.dram_tensor("ident", [128, 128], F32, kind="ExternalInput").ap()
    amat_d = nc.dram_tensor("amat", [T, T], BF16, kind="ExternalInput").ap()
    out_d = nc.dram_tensor("out", [NSEQ * T, 1024], F32, kind="ExternalOutput").ap()
    w1b = nc.dram_tensor("w1b", [NE * 1024, 4096], BF16).ap()
    w2b = nc.dram_tensor("w2b", [NE * 4096, 1024], BF16).ap()

    A = nc.alloc_sbuf_tensor
    x_sb = A("x_sb", [P, DC * T], F32).ap()
    xnT = A("xnT", [P, DC * T], BF16).ap()
    w1buf = [A(f"w1buf{i}", [P, DC * 512], BF16).ap() for i in range(2)]
    w2buf = [A(f"w2buf{i}", [P, 16 * 128], BF16).ap() for i in range(2)]
    xstage = [A(f"xstage{i}", [P, 1024], F32).ap() for i in range(2)]
    for i in range(2):
        w2buf.append(nc.alloc_sbuf_tensor_at(
            f"w2buf{2 + i}", [P, 16 * 128], BF16,
            offset=nc.lookup_mloc(xstage[i].tensor).addr).ap())
    tmpv = A("tmpv", [P, 1024], F32).ap()
    coef_rep = A("coef_rep", [P, HT], F32).ap()
    coefcols = A("coefcols", [P, 2 * NTH * NE], F32).ap()
    b1t = A("b1t_sb", [P, 256], F32).ap()
    b2t = A("b2t_sb", [P, 64], F32).ap()
    id32 = A("id32", [P, 128], F32).ap()
    identbf = A("identbf", [P, 128], BF16).ap()
    onesb = A("onesb", [P, 128], BF16).ap()
    ones32 = A("ones32", [P, 1], F32).ap()
    wred_sb = A("wred_sb", [P, DC * 16], BF16).ap()
    wg_sb = A("wg_sb", [P, 16], F32).ap()
    wgsq = A("wgsq", [P, 16], F32).ap()
    wgn = A("wgn", [P, 16], F32).ap()
    wgnT = A("wgnT", [P, 8], F32).ap()
    nrm = A("nrm", [P, 2], F32).ap()
    diagb = [A(f"diagb{i}", [P, 128], BF16).ap() for i in range(2)]
    dlog_sb = A("dlog_sb", [P, 1], F32).ap()
    sigd = A("sigd", [P, 1], F32).ap()
    logd = A("logd", [P, 1], F32).ap()
    oscale_sb = A("oscale_sb", [P, 1], F32).ap()
    ceps = A("ceps", [P, 1], F32).ap()
    gexp = A("gexp", [P, 8], F32).ap()
    negm = A("negm", [P, 1], F32).ap()
    gsum = A("gsum", [P, 1], F32).ap()
    rs = A("rs", [P, 1], F32).ap()
    is8 = A("is8", [P, 8], F32).ap()

    AR = (nc.sbuf_bytes_remaining - 64) // 32 * 32
    assert AR >= 66560, AR
    arena = A("arena", [P, AR], mybir.dt.uint8)
    base = nc.lookup_mloc(arena).addr

    def at(name, shape, dtype, off):
        nb = int(np.prod(shape[1:])) * mybir.dt.size(dtype)
        assert off + nb <= AR, (name, off, nb)
        return nc.alloc_sbuf_tensor_at(name, shape, dtype, offset=base + off).ap()

    hT = at("hT", [P, FC * HT], BF16, 0)
    xsq = at("xsq", [P, DC * HT], BF16, 0)
    mrow = at("mrow", [1, HT], F32, 16384)
    srow = at("srow", [1, HT], F32, 20480)
    rrow = at("rrow", [1, HT], F32, 24576)
    mbf = at("mbf", [1, HT], BF16, 28672)
    rbf = at("rbf", [1, HT], BF16, 30720)
    redT = at("redT", [P, HT], F32, 32768)
    QT = at("QT", [P, 2 * T], BF16, 0)
    KT = at("KT", [P, 2 * T], BF16, 8192)
    V_sb = at("V_sb", [P, NT * 256], BF16, 16384)
    SdT = at("SdT", [P, NT * TBL], BF16, 24576)
    rT = at("rT", [P, 2 * TBL], BF16, 40960)
    projbuf = at("projbuf", [P, DC * 256], BF16, 43008)
    owT = at("owT", [P, 2 * 1024], BF16, 43008)
    bsT = at("bsT", [P, 2 * 1024], BF16, 47104)
    cT4 = at("cT4", [P, 4 * 512], BF16, 51200)
    abuf = [at(f"abuf{i}", [P, TBL], BF16, 55296 + i * 1024) for i in range(2)]
    mexp = [at(f"mexp{i}", [P, TBL], F32, 57344 + i * 2048) for i in range(2)]
    mexpd = [at(f"mexpd{i}", [P, TBL], F32, 61440 + i * 2048) for i in range(2)]

    Pb = [nc.alloc_psum_tensor(f"P{i}", [P, 512], F32).ap() for i in range(4)]
    Yb = [nc.alloc_psum_tensor(f"Y{i}", [P, 1024], F32).ap() for i in range(2)]

    b = B(nc)
    for s in ("SP", "SV", "SC", "SX0", "SX1", "SO0", "SO1", "SG", "SA20", "SA21", "SCV", "SW10", "SW11", "SW20", "SW21", "SW22", "SW23"):
        b.sem(s)
    SW1 = ("SW10", "SW11")
    SW2 = ("SW20", "SW21", "SW22", "SW23")

    free_at = {}

    def reuse_wait(eng, *keys):
        for k in keys:
            ev = free_at.get(k)
            if ev:
                b.wait(eng, ev[0], ev[1])

    def set_free(key, sname, val):
        free_at[key] = (sname, val)


    def small(fn):
        v = b.op("gpsimd", fn, "SG", 16)
        b.wait("gpsimd", "SG", v)
        return v

    small(lambda E: E.dma_start(identbf[:, :], ident_d[:, :]))
    small(lambda E: E.dma_start(
        wred_sb[:, :], wred_d.rearrange("(dc p) g -> p dc g", p=P)))
    small(lambda E: E.dma_start(b1t[:, :], b1t_d[:, :]))
    small(lambda E: E.dma_start(b2t[:, :], b2t_d[:, :]))
    small(lambda E: E.dma_start(dlog_sb[:, :], dlog_d[:, :]))
    small(lambda E: E.dma_start(oscale_sb[:, :], oscale_d[:, :]))
    sg_smalls = small(lambda E: E.dma_start(wg_sb[0:8, :], wg_d[:, :]))

    sg_id = small(lambda E: E.dma_start(id32[:, :], ident_d[:, :]))

    cast_idx = {}
    for e in range(NE):
        for sl in range(8):
            cast_idx[("w1", e, sl)] = b.op("gpsimd", lambda E, e=e, sl=sl: E.dma_start(
                w1b[e * 1024:(e + 1) * 1024, sl * 512:(sl + 1) * 512],
                w1[e * 1024:(e + 1) * 1024, sl * 512:(sl + 1) * 512]), "SCV", 16)
            b.wait("gpsimd", "SCV", b.v("SCV"))
        for ck in range(8):
            cast_idx[("w2", e, ck)] = b.op("gpsimd", lambda E, e=e, ck=ck: E.dma_start(
                w2b[e * 4096 + ck * 512:e * 4096 + (ck + 1) * 512, :],
                w2[e * 4096 + ck * 512:e * 4096 + (ck + 1) * 512, :]), "SCV", 16)
            b.wait("gpsimd", "SCV", b.v("SCV"))


    b.op("vector", lambda E: E.memset(onesb[:, :], 1.0), "SV")
    b.op("vector", lambda E: E.memset(ones32[:, :], 1.0), "SV")
    b.op("vector", lambda E: E.memset(ceps[:, :], 1e-5), "SV")
    sv_memset = b.v("SV")

    b.wait("scalar", "SG", sg_smalls)
    sc1 = b.op("scalar", lambda E: E.activation(sigd[:, :], dlog_sb[:, :], AF.Sigmoid), "SC")
    b.wait("scalar", "SC", sc1)
    sc2 = b.op("scalar", lambda E: E.activation(logd[:, :], sigd[:, :], AF.Ln), "SC")
    b.wait("vector", "SC", sc2)
    sv_logd = b.op("vector", lambda E: E.tensor_scalar_min(logd[:, :], logd[:, :], -1e-7), "SV")

    b.wait("scalar", "SG", sg_smalls)
    sc3 = b.op("scalar", lambda E: E.square(wgsq[0:8, :], wg_sb[0:8, :]), "SC")
    b.wait("vector", "SC", sc3)
    sv1 = b.op("vector", lambda E: E.reduce_sum(nrm[0:8, 0:1], wgsq[0:8, :], axis=AX.X), "SV")
    b.wait("scalar", "SV", sv1)
    sc4 = b.op("scalar", lambda E: E.activation(nrm[0:8, 1:2], nrm[0:8, 0:1], AF.Sqrt), "SC")
    b.wait("vector", "SC", sc4)
    sv2 = b.op("vector", lambda E: E.tensor_scalar_max(nrm[0:8, 1:2], nrm[0:8, 1:2], 1e-4), "SV")
    b.wait("vector", "SV", sv2)
    sv3 = b.op("vector", lambda E: E.reciprocal(nrm[0:8, 0:1], nrm[0:8, 1:2]), "SV")
    b.wait("vector", "SV", sv3)
    sv4 = b.op("vector", lambda E: E.tensor_scalar(
        wgn[0:8, :], wg_sb[0:8, :], nrm[0:8, 0:1], None, ALU.mult), "SV")
    b.wait("tensor", "SV", sv4)
    b.wait("tensor", "SG", sg_id)
    sp1 = b.op("tensor", lambda E: E.transpose(Pb[0][0:16, 0:8], wgn[0:8, 0:16], id32[0:8, 0:8]), "SP")
    b.wait("vector", "SP", sp1)
    sv5 = b.op("vector", lambda E: E.tensor_copy(wgnT[0:16, 0:8], Pb[0][0:16, 0:8]), "SV")
    set_free("P0", "SV", sv5)

    x3 = x_sb.rearrange("p (dc t) -> p dc t", t=T)

    def load_seq(seq):
        sc_last = None
        for tile in range(NT):
            par = tile % 2
            reuse_wait("sync", f"xstage{par}")
            sxn = f"SX{par}"
            b.op("sync", lambda E, seq=seq, tile=tile, par=par: E.dma_start(
                xstage[par][:, :],
                x_in[seq * T + tile * 128:seq * T + (tile + 1) * 128, :]), sxn, 16)
            b.wait("tensor", sxn, b.v(sxn))
            if tile == 0:
                b.wait("tensor", "SG", sg_id)
            reuse_wait("tensor", "P0", "P1")
            sp = None
            for dc in range(DC):
                pp = Pb[0] if dc < 4 else Pb[1]
                sp = b.op("tensor", lambda E, dc=dc, par=par, pp=pp: E.transpose(
                    pp[:, (dc % 4) * 128:(dc % 4 + 1) * 128],
                    xstage[par][:, dc * 128:(dc + 1) * 128], id32[:, :]), "SP")
            set_free(f"xstage{par}", "SP", sp)
            b.wait("scalar", "SP", sp)
            for g in range(2):
                sc_last = b.op("scalar", lambda E, g=g, tile=tile: E.copy(
                    x3[:, g * 4:(g + 1) * 4, tile * 128:(tile + 1) * 128],
                    Pb[g][:, 0:512].rearrange("p (a t) -> p a t", t=128)), "SC")
            set_free("P0", "SC", sc_last)
            set_free("P1", "SC", sc_last)
        spend = b.v("SP")
        set_free("w2buf2", "SP", spend)
        set_free("w2buf3", "SP", spend)
        return ("SC", sc_last)

    def ln_half(seq, subl, half, routing, ev_x):
        t0 = half * HT
        b.wait("scalar", "SP", b.v("SP"))
        b.wait("scalar", ev_x[0], ev_x[1])
        b.wait("tensor", ev_x[0], ev_x[1])
        sc_sq = None
        for dc in range(DC):
            sc_sq = b.op("scalar", lambda E, dc=dc, t0=t0: E.square(
                xsq[:, dc * HT:(dc + 1) * HT], x_sb[:, dc * T + t0:dc * T + t0 + HT]), "SC")
        reuse_wait("tensor", "Y0", "Y1")
        b.wait("tensor", "SV", sv_memset)
        for dc in range(DC):
            for g in range(2):
                b.op("tensor", lambda E, dc=dc, g=g, t0=t0: E.matmul(
                    Yb[0][0:1, g * 512:(g + 1) * 512], ones32[:, 0:1],
                    x_sb[:, dc * T + t0 + g * 512:dc * T + t0 + (g + 1) * 512],
                    start=(dc == 0), stop=(dc == DC - 1)), "SP")
        b.wait("tensor", "SC", sc_sq)
        sp_st = None
        for dc in range(DC):
            for g in range(2):
                sp_st = b.op("tensor", lambda E, dc=dc, g=g: E.matmul(
                    Yb[1][0:1, g * 512:(g + 1) * 512], onesb[:, 0:1],
                    xsq[:, dc * HT + g * 512:dc * HT + (g + 1) * 512],
                    start=(dc == 0), stop=(dc == DC - 1)), "SP")
        b.wait("vector", "SP", sp_st)
        sv_m = b.op("vector", lambda E: E.tensor_scalar(
            mrow[0:1, :], Yb[0][0:1, 0:HT], 1.0 / 1024.0, None, ALU.mult), "SV")
        b.wait("scalar", "SV", sv_m)
        sc_msq = b.op("scalar", lambda E: E.square(srow[0:1, :], mrow[0:1, :]), "SC")
        b.wait("vector", "SC", sc_msq)
        sv_var = b.op("vector", lambda E: E.scalar_tensor_tensor(
            rrow[0:1, :], Yb[1][0:1, 0:HT], 1.0 / 1024.0, srow[0:1, :],
            ALU.mult, ALU.subtract), "SV")
        b.wait("scalar", "SV", sv_var)
        sc_sd = b.op("scalar", lambda E: E.activation(
            srow[0:1, :], rrow[0:1, :], AF.Sqrt, bias=ceps[0:1, :]), "SC")
        b.wait("vector", "SC", sc_sd)
        sv_r = b.op("vector", lambda E: E.reciprocal(rrow[0:1, :], srow[0:1, :]), "SV")
        b.wait("vector", "SV", sv_r)
        sv_mb = b.op("vector", lambda E: E.tensor_copy(mbf[0:1, :], mrow[0:1, :]), "SV")
        sv_rb = b.op("vector", lambda E: E.tensor_copy(rbf[0:1, :], rrow[0:1, :]), "SV")
        b.wait("tensor", "SV", sv_rb)
        for g in range(2):
            b.op("tensor", lambda E, g=g: E.matmul(
                Yb[0][:, g * 512:(g + 1) * 512], onesb[0:1, 0:128],
                mbf[0:1, g * 512:(g + 1) * 512], start=True, stop=True), "SP")
        sp_rep = None
        for g in range(2):
            sp_rep = b.op("tensor", lambda E, g=g: E.matmul(
                Yb[1][:, g * 512:(g + 1) * 512], onesb[0:1, 0:128],
                rbf[0:1, g * 512:(g + 1) * 512], start=True, stop=True), "SP")
        b.wait("vector", "SP", sp_rep)
        sv_xn = None
        for dc in range(DC):
            reuse_wait("vector", "tmpv")
            sv_t = b.op("vector", lambda E, dc=dc, t0=t0: E.tensor_tensor(
                tmpv[:, 0:HT], x_sb[:, dc * T + t0:dc * T + t0 + HT],
                Yb[0][:, 0:HT], ALU.subtract), "SV")
            b.wait("vector", "SV", sv_t)
            sv_xn = b.op("vector", lambda E, dc=dc, t0=t0: E.tensor_tensor(
                xnT[:, dc * T + t0:dc * T + t0 + HT], tmpv[:, 0:HT],
                Yb[1][:, 0:HT], ALU.mult), "SV")
            set_free("tmpv", "SV", sv_xn)
        set_free("Y0", "SV", sv_xn)
        set_free("Y1", "SV", sv_xn)
        if not routing:
            return ("SV", sv_xn)

        reuse_wait("tensor", "P0", "P1")
        b.wait("tensor", "SV", sv_xn)
        sp_red = None
        for dc in range(DC):
            for g in range(2):
                sp_red = b.op("tensor", lambda E, dc=dc, g=g, t0=t0: E.matmul(
                    Pb[g][0:16, 0:512], wred_sb[:, dc * 16:(dc + 1) * 16],
                    xnT[:, dc * T + t0 + g * 512:dc * T + t0 + (g + 1) * 512],
                    start=(dc == 0), stop=(dc == DC - 1)), "SP")
        b.wait("vector", "SP", sp_red)
        sv_red = None
        for g in range(2):
            sv_red = b.op("vector", lambda E, g=g: E.tensor_copy(
                redT[0:16, g * 512:(g + 1) * 512], Pb[g][0:16, 0:512]), "SV")
        set_free("P0", "SV", sv_red)
        set_free("P1", "SV", sv_red)
        co = half * NTH * NE
        sc_ex_prev = None
        sv_cf_prev = None
        sv_cf = None
        for tile in range(NTH):
            key = f"P{2 + tile % 2}"
            pp = Pb[2 + tile % 2]
            reuse_wait("tensor", key)
            b.wait("tensor", "SV", sv_red)
            sp_lg = b.op("tensor", lambda E, tile=tile, pp=pp: E.matmul(
                pp[:, tile * 8:tile * 8 + 8], redT[0:16, tile * 128:(tile + 1) * 128],
                wgnT[0:16, 0:8], start=True, stop=True), "SP")
            b.wait("vector", "SP", sp_lg)
            if sc_ex_prev:
                b.wait("vector", "SC", sc_ex_prev)
            sv_nm = b.op("vector", lambda E, tile=tile, pp=pp: E.reduce_max(
                negm[:, 0:1], pp[:, tile * 8:tile * 8 + 8], axis=AX.X, negate=True), "SV")
            b.wait("scalar", "SV", sv_nm)
            sc_ex = b.op("scalar", lambda E, tile=tile, pp=pp: E.activation(
                gexp[:, 0:8], pp[:, tile * 8:tile * 8 + 8], AF.Exp, bias=negm[:, 0:1]), "SC")
            sc_ex_prev = sc_ex
            b.wait("vector", "SC", sc_ex)
            sv_sm = b.op("vector", lambda E: E.reduce_sum(
                gsum[:, 0:1], gexp[:, 0:8], axis=AX.X), "SV")
            b.wait("vector", "SV", sv_sm)
            if sv_cf_prev:
                b.wait("vector", "SV", sv_cf_prev)
            sv_rs = b.op("vector", lambda E: E.reciprocal(rs[:, 0:1], gsum[:, 0:1]), "SV")
            sv_is = b.op("vector", lambda E, tile=tile, pp=pp: E.tensor_scalar(
                is8[:, 0:8], pp[:, tile * 8:tile * 8 + 8], negm[:, 0:1], 0.0,
                ALU.add, ALU.is_ge), "SV")
            b.wait("vector", "SV", sv_is)
            sv_cf = b.op("vector", lambda E, tile=tile, co=co: E.tensor_scalar(
                coefcols[:, co + tile * 8:co + tile * 8 + 8], is8[:, 0:8],
                rs[:, 0:1], 0.5, ALU.mult, ALU.mult), "SV")
            sv_cf_prev = sv_cf
            set_free(key, "SV", sv_cf)
        return ("SV", sv_cf)

    wdma = {"w1n": [0, 0], "w2n": [0, 0, 0, 0],
            "w1free": [None, None], "w2free": [None, None, None, None]}

    def issue_w1(e, sl, first_use):
        par = sl % 2
        ev = wdma["w1free"][par]
        if ev:
            b.wait("sync", ev[0], ev[1])
        if first_use:
            b.wait("sync", "SCV", cast_idx[("w1", e, sl)])
        wdma["w1n"][par] += 1
        b.op("sync", lambda E, e=e, sl=sl, par=par: E.dma_start(
            w1buf[par][:, :],
            w1b[e * 1024:(e + 1) * 1024, sl * 512:(sl + 1) * 512]
            .rearrange("(dc p) f -> p dc f", p=P)), SW1[par], 16)
        return wdma["w1n"][par]

    def issue_w2(e, vc, ch, first_use):
        par = (vc * 2 + ch) % 4
        ev = wdma["w2free"][par]
        if ev:
            b.wait("sync", ev[0], ev[1])
        if first_use:
            b.wait("sync", "SCV", cast_idx[("w2", e, ch * 4 + 3)])
        wdma["w2n"][par] += 1
        if par >= 2:
            reuse_wait("sync", f"w2buf{par}")
        b.op("sync", lambda E, e=e, vc=vc, ch=ch, par=par: E.dma_start(
            w2buf[par][:, :],
            w2b[e * 4096 + ch * 2048:e * 4096 + (ch + 1) * 2048,
                vc * 128:(vc + 1) * 128]
            .rearrange("(cl p) v -> p cl v", p=P)), SW2[par], 16)
        return wdma["w2n"][par]

    def ffn_half(seq, subl, half, first_use):
        t0 = half * HT
        co = half * NTH * NE
        sv_last = b.v("SV")
        for e in range(NE):
            fu = (seq == 0)
            w1rdy = {}
            w1rdy[0] = issue_w1(e, 0, fu)
            w1rdy[1] = issue_w1(e, 1, fu)
            w2rdy = {}
            for (pv, pc) in ((0, 0), (0, 1), (1, 0), (1, 1)):
                w2rdy[(pv, pc)] = issue_w2(e, pv, pc, fu)

            sp_cf_last = None
            for c in range(FC):
                sl, cl, spar = c // 4, c % 4, (c // 4) % 2
                bk = (Pb[0], Pb[1]) if c % 2 == 0 else (Pb[2], Pb[3])
                bkeys = ("P0", "P1") if c % 2 == 0 else ("P2", "P3")
                if cl == 0:
                    b.wait("tensor", SW1[spar], 16 * w1rdy[sl])
                reuse_wait("tensor", *bkeys)
                sp_h = None
                for k in range(DC):
                    for tg in range(2):
                        sp_h = b.op("tensor", lambda E, k=k, tg=tg, cl=cl, spar=spar, bk=bk, t0=t0: E.matmul(
                            bk[tg][:, 0:512],
                            w1buf[spar][:, k * 512 + cl * 128:k * 512 + (cl + 1) * 128],
                            xnT[:, k * T + t0 + tg * 512:k * T + t0 + (tg + 1) * 512],
                            start=(k == 0), stop=(k == DC - 1)), "SP")
                if cl == 3:
                    wdma["w1free"][sl % 2] = ("SP", sp_h)
                    if sl + 2 < 8:
                        w1rdy[sl + 2] = issue_w1(e, sl + 2, fu)
                if c == 8:
                    reuse_wait("tensor", "Y0")
                if 8 <= c < 16:
                    tile = c - 8
                    par = tile % 2
                    reuse_wait("vector", f"diag{par}")
                    sv_d = b.op("vector", lambda E, tile=tile, par=par, e=e, co=co: E.tensor_scalar(
                        diagb[par][:, :], identbf[:, :],
                        coefcols[:, co + tile * 8 + e:co + tile * 8 + e + 1], None, ALU.mult), "SV")
                    b.wait("tensor", "SV", sv_d)
                    sp_cf_last = b.op("tensor", lambda E, tile=tile, par=par: E.matmul(
                        Yb[0][:, tile * 128:(tile + 1) * 128], onesb[:, 0:128],
                        diagb[par][:, :], start=True, stop=True), "SP")
                    set_free(f"diag{par}", "SP", sp_cf_last)
                b.wait("scalar", "SP", sp_h)
                sc_g = None
                for tg in range(2):
                    sc_g = b.op("scalar", lambda E, c=c, tg=tg, e=e, bk=bk: E.activation(
                        hT[:, c * HT + tg * 512:c * HT + (tg + 1) * 512], bk[tg][:, 0:512],
                        AF.Gelu, bias=b1t[:, e * 32 + c:e * 32 + c + 1]), "SC")
                set_free(bkeys[0], "SC", sc_g)
                set_free(bkeys[1], "SC", sc_g)
            sc_h_all = b.v("SC")
            reuse_wait("vector", "coef_rep")
            b.wait("vector", "SP", sp_cf_last)
            sv_coef = b.op("vector", lambda E: E.tensor_copy(
                coef_rep[:, 0:1024], Yb[0][:, 0:1024]), "SV")
            set_free("Y0", "SV", sv_coef)
            b.wait("tensor", "SC", sc_h_all)

            sv_x = None
            for vc in range(NE):
                yb = Yb[vc % 2]
                ykey = f"Y{vc % 2}"
                reuse_wait("tensor", ykey)
                sp_y = None
                for ch in range(2):
                    par2 = (vc * 2 + ch) % 4
                    b.wait("tensor", SW2[par2], 16 * w2rdy[(vc, ch)])
                    for cl in range(16):
                        c = ch * 16 + cl
                        for tg in range(2):
                            sp_y = b.op("tensor", lambda E, cl=cl, c=c, tg=tg, par2=par2, yb=yb: E.matmul(
                                yb[:, tg * 512:(tg + 1) * 512],
                                w2buf[par2][:, cl * 128:(cl + 1) * 128],
                                hT[:, c * HT + tg * 512:c * HT + (tg + 1) * 512],
                                start=(c == 0), stop=(c == FC - 1)), "SP")
                    wdma["w2free"][par2] = ("SP", sp_y)
                    if vc + 2 < NE:
                        w2rdy[(vc + 2, ch)] = issue_w2(e, vc + 2, ch, fu)
                b.wait("vector", "SP", sp_y)
                b.wait("vector", "SV", sv_coef)
                reuse_wait("vector", "tmpv")
                sv_a = b.op("vector", lambda E, vc=vc, e=e, yb=yb: E.tensor_scalar(
                    tmpv[:, 0:HT], yb[:, 0:HT], b2t[:, e * 8 + vc:e * 8 + vc + 1],
                    None, ALU.add), "SV")
                set_free(ykey, "SV", sv_a)
                b.wait("vector", "SV", sv_a)
                sv_b = b.op("vector", lambda E: E.tensor_tensor(
                    tmpv[:, 0:HT], tmpv[:, 0:HT], coef_rep[:, 0:HT], ALU.mult), "SV")
                b.wait("vector", "SV", sv_b)
                sv_x = b.op("vector", lambda E, vc=vc, t0=t0: E.tensor_tensor(
                    x_sb[:, vc * T + t0:vc * T + t0 + HT],
                    x_sb[:, vc * T + t0:vc * T + t0 + HT], tmpv[:, 0:HT], ALU.add), "SV")
                set_free("tmpv", "SV", sv_x)
        spend = b.v("SP")
        set_free("xstage0", "SP", spend)
        set_free("xstage1", "SP", spend)
        return ("SV", sv_x)

    def attention(seq):
        spbar = b.v("SP")
        b.wait("gpsimd", "SP", spbar)
        b.wait("scalar", "SP", spbar)
        b.wait("vector", "SP", spbar)
        v = b.op("gpsimd", lambda E: E.dma_start(
            bsT[:, :], basisT_d.rearrange("(nb p) v -> p nb v", p=P)), "SG", 16)
        b.wait("gpsimd", "SG", v)
        for i in range(4):
            v = b.op("gpsimd", lambda E, i=i: E.dma_start(
                cT4[:, i * 512:(i + 1) * 512],
                coefT_d[i].rearrange("(nb p) c -> p nb c", p=P)), "SG", 16)
            b.wait("gpsimd", "SG", v)
        sg_attn = b.v("SG")

        def build_proj(which):
            b.wait("tensor", "SG", sg_attn)
            sc = None
            for dc in range(DC):
                key = f"P{dc % 2}"
                pp = Pb[dc % 2]
                reuse_wait("tensor", key)
                sp = None
                for nb in range(2):
                    sp = b.op("tensor", lambda E, dc=dc, nb=nb, pp=pp, w=which: E.matmul(
                        pp[:, 0:256], bsT[:, nb * 1024 + dc * 128:nb * 1024 + (dc + 1) * 128],
                        cT4[:, w * 512 + nb * 256:w * 512 + (nb + 1) * 256],
                        start=(nb == 0), stop=(nb == 1)), "SP")
                b.wait("scalar", "SP", sp)
                sc = b.op("scalar", lambda E, dc=dc, pp=pp: E.copy(
                    projbuf[:, dc * 256:(dc + 1) * 256], pp[:, 0:256]), "SC")
                set_free(key, "SC", sc)
            return sc

        def build_qkt(dst, ev_proj):
            b.wait("tensor", "SC", ev_proj)
            sc = None
            for cc in range(2):
                reuse_wait("tensor", "Y0", "Y1")
                sp = None
                for dc in range(DC):
                    for tg in range(4):
                        yb = Yb[tg // 2]
                        sp = b.op("tensor", lambda E, cc=cc, dc=dc, tg=tg, yb=yb: E.matmul(
                            yb[:, (tg % 2) * 512:(tg % 2 + 1) * 512],
                            projbuf[:, dc * 256 + cc * 128:dc * 256 + (cc + 1) * 128],
                            xnT[:, dc * T + tg * 512:dc * T + (tg + 1) * 512],
                            start=(dc == 0), stop=(dc == DC - 1)), "SP")
                b.wait("scalar", "SP", sp)
                for g in range(2):
                    sc = b.op("scalar", lambda E, cc=cc, g=g, dst=dst: E.copy(
                        dst[:, cc * T + g * 1024:cc * T + (g + 1) * 1024],
                        Yb[g][:, 0:1024]), "SC")
                set_free("Y0", "SC", sc)
                set_free("Y1", "SC", sc)
            return sc

        ev = build_proj(0)
        sc_qt = build_qkt(QT, ev)
        ev = build_proj(1)
        sc_kt = build_qkt(KT, ev)
        ev_v = build_proj(2)
        b.wait("tensor", "SC", ev_v)
        sc_v = None
        for st in range(NT):
            key = f"P{2 + st % 2}"
            pp = Pb[2 + st % 2]
            reuse_wait("tensor", key)
            sp = None
            for dc in range(DC):
                sp = b.op("tensor", lambda E, st=st, dc=dc, pp=pp: E.matmul(
                    pp[:, 0:256], xnT[:, dc * T + st * 128:dc * T + (st + 1) * 128],
                    projbuf[:, dc * 256:(dc + 1) * 256],
                    start=(dc == 0), stop=(dc == DC - 1)), "SP")
            b.wait("scalar", "SP", sp)
            sc_v = b.op("scalar", lambda E, st=st, pp=pp: E.copy(
                V_sb[:, st * 256:(st + 1) * 256], pp[:, 0:256]), "SC")
            set_free(key, "SC", sc_v)
        sc_ow = None
        for cc in range(2):
            reuse_wait("tensor", "Y0")
            b.wait("tensor", "SC", sc_v)
            sp = None
            for nb in range(2):
                for g in range(2):
                    sp = b.op("tensor", lambda E, cc=cc, nb=nb, g=g: E.matmul(
                        Yb[0][:, g * 512:(g + 1) * 512],
                        cT4[:, 3 * 512 + nb * 256 + cc * 128:3 * 512 + nb * 256 + (cc + 1) * 128],
                        bsT[:, nb * 1024 + g * 512:nb * 1024 + (g + 1) * 512],
                        start=(nb == 0), stop=(nb == 1)), "SP")
            b.wait("scalar", "SP", sp)
            sc_ow = b.op("scalar", lambda E, cc=cc: E.copy(
                owT[:, cc * 1024:(cc + 1) * 1024], Yb[0][:, 0:1024]), "SC")
            set_free("Y0", "SC", sc_ow)

        sv_o = None
        for tb in range(NTB):
            sd_done = {}
            for st in range(NT):
                par = st % 2
                reuse_wait("scalar", f"abuf{par}")
                san = f"SA2{par}"
                b.op("scalar", lambda E, st=st, tb=tb, par=par: E.dma_start(
                    abuf[par][:, :],
                    amat_d[st * 128:(st + 1) * 128, tb * TBL:(tb + 1) * TBL]), san, 16)
                b.wait("vector", san, b.v(san))
                b.wait("vector", "SV", sv_logd)
                reuse_wait("vector", f"mexp{par}")
                sv_m1 = b.op("vector", lambda E, par=par: E.tensor_scalar(
                    mexp[par][:, :], abuf[par][:, :], logd[:, 0:1], None, ALU.mult), "SV")
                set_free(f"abuf{par}", "SV", sv_m1)
                b.wait("scalar", "SV", sv_m1)
                reuse_wait("scalar", f"mexpd{par}")
                sc_m2 = b.op("scalar", lambda E, par=par: E.activation(
                    mexpd[par][:, :], mexp[par][:, :], AF.Exp), "SC")
                set_free(f"mexp{par}", "SC", sc_m2)
                key = f"P{st % 2}"
                pp = Pb[st % 2]
                reuse_wait("tensor", key)
                if st == 0 and tb == 0:
                    b.wait("tensor", "SC", sc_kt)
                sp_s = None
                for cc in range(2):
                    sp_s = b.op("tensor", lambda E, st=st, tb=tb, cc=cc, pp=pp: E.matmul(
                        pp[:, 0:TBL], KT[:, cc * T + st * 128:cc * T + (st + 1) * 128],
                        QT[:, cc * T + tb * TBL:cc * T + (tb + 1) * TBL],
                        start=(cc == 0), stop=(cc == 1)), "SP")
                b.wait("vector", "SC", sc_m2)
                b.wait("vector", "SP", sp_s)
                sv_sd = b.op("vector", lambda E, st=st, par=par, pp=pp: E.tensor_tensor(
                    SdT[:, st * TBL:(st + 1) * TBL], pp[:, 0:TBL], mexpd[par][:, :],
                    ALU.mult), "SV")
                set_free(key, "SV", sv_sd)
                set_free(f"mexpd{par}", "SV", sv_sd)
                sd_done[st] = sv_sd
            sc_rt = None
            for cc in range(2):
                key = f"P{2 + cc}"
                pp = Pb[2 + cc]
                reuse_wait("tensor", key)
                sp_r = None
                for st in range(NT):
                    b.wait("tensor", "SV", sd_done[st])
                    sp_r = b.op("tensor", lambda E, cc=cc, st=st, pp=pp: E.matmul(
                        pp[:, 0:TBL], V_sb[:, st * 256 + cc * 128:st * 256 + (cc + 1) * 128],
                        SdT[:, st * TBL:(st + 1) * TBL],
                        start=(st == 0), stop=(st == NT - 1)), "SP")
                b.wait("scalar", "SP", sp_r)
                sc_rt = b.op("scalar", lambda E, cc=cc, pp=pp: E.copy(
                    rT[:, cc * TBL:(cc + 1) * TBL], pp[:, 0:TBL]), "SC")
                set_free(key, "SC", sc_rt)
            for vcq in range(NE):
                ykey = f"Y{vcq % 2}"
                yb = Yb[vcq % 2]
                reuse_wait("tensor", ykey)
                b.wait("tensor", "SC", sc_rt)
                sp_o = None
                for cc in range(2):
                    sp_o = b.op("tensor", lambda E, vcq=vcq, cc=cc, yb=yb: E.matmul(
                        yb[:, 0:TBL], owT[:, cc * 1024 + vcq * 128:cc * 1024 + (vcq + 1) * 128],
                        rT[:, cc * TBL:(cc + 1) * TBL],
                        start=(cc == 0), stop=(cc == 1)), "SP")
                b.wait("vector", "SP", sp_o)
                sv_o = b.op("vector", lambda E, vcq=vcq, tb=tb, yb=yb: E.scalar_tensor_tensor(
                    x_sb[:, vcq * T + tb * TBL:vcq * T + (tb + 1) * TBL], yb[:, 0:TBL],
                    oscale_sb[:, 0:1],
                    x_sb[:, vcq * T + tb * TBL:vcq * T + (tb + 1) * TBL],
                    ALU.mult, ALU.add), "SV")
                set_free(ykey, "SV", sv_o)
        return ("SV", sv_o)

    def store_seq(seq, ev_final):
        b.wait("tensor", ev_final[0], ev_final[1])
        for tile in range(NT):
            par = tile % 2
            reuse_wait("tensor", "P0", "P1")
            sp = None
            for dc in range(DC):
                pp = Pb[0] if dc < 4 else Pb[1]
                sp = b.op("tensor", lambda E, dc=dc, tile=tile, pp=pp: E.transpose(
                    pp[:, (dc % 4) * 128:(dc % 4 + 1) * 128],
                    x_sb[:, dc * T + tile * 128:dc * T + (tile + 1) * 128], id32[:, :]), "SP")
            b.wait("scalar", "SP", sp)
            reuse_wait("scalar", f"xstage{par}")
            sc = None
            for g in range(2):
                sc = b.op("scalar", lambda E, g=g, par=par: E.copy(
                    xstage[par][:, g * 512:(g + 1) * 512], Pb[g][:, 0:512]), "SC")
            set_free("P0", "SC", sc)
            set_free("P1", "SC", sc)
            b.wait("gpsimd", "SC", sc)
            son = f"SO{par}"
            b.op("gpsimd", lambda E, seq=seq, tile=tile, par=par: E.dma_start(
                out_d[seq * T + tile * 128:seq * T + (tile + 1) * 128, :],
                xstage[par][:, :]), son, 16)
            set_free(f"xstage{par}", son, b.v(son))
            set_free(f"w2buf{2 + par}", son, b.v(son))

    for seq in range(NSEQ):
        ev = load_seq(seq)
        for subl in range(3):
            if subl != 1:
                ev_r = {}
                for half in range(2):
                    ev_r[half] = ln_half(seq, subl, half, True, ev)
                for half in range(2):
                    ev = ffn_half(seq, subl, half,
                                  first_use=(seq == 0 and subl == 0))
            else:
                for half in range(2):
                    ev = ln_half(seq, subl, half, False, ev)
                ev = attention(seq)
        store_seq(seq, ev)

    b.wait("gpsimd", "SO0", b.v("SO0"))
    b.wait("gpsimd", "SO1", b.v("SO1"))
    b.wait("sync", "SO0", b.v("SO0"))
    b.wait("sync", "SO1", b.v("SO1"))

    with nc.Block() as block:
        @block.tensor
        def _(E):
            b.replay("tensor", E)

        @block.vector
        def _(E):
            b.replay("vector", E)

        @block.scalar
        def _(E):
            b.replay("scalar", E)

        @block.sync
        def _(E):
            b.replay("sync", E)

        @block.gpsimd
        def _(E):
            b.replay("gpsimd", E)

    b.close_sems()
    n_inst = {k: len(b.plan[k]) for k in b.plan}
    return nc, n_inst


_CACHE = {}


def kernel(**inputs):
    x = np.asarray(inputs["L0_attn_x"], dtype=np.float32)
    basis = np.asarray(inputs["L0_attn_basis"], dtype=np.float32)
    qc = np.asarray(inputs["L0_attn_q_coeffs"], dtype=np.float32)
    kc = np.asarray(inputs["L0_attn_k_coeffs"], dtype=np.float32)
    vcc = np.asarray(inputs["L0_attn_v_coeffs"], dtype=np.float32)
    oc = np.asarray(inputs["L0_attn_o_coeffs"], dtype=np.float32)
    dl = np.asarray(inputs["L0_attn_decay_logit"], dtype=np.float32)
    osc = np.asarray(inputs["L0_attn_out_scale"], dtype=np.float32)
    wred = np.asarray(inputs["L0_ffn_w_red"], dtype=np.float32)
    wg = np.asarray(inputs["L0_ffn_wg"], dtype=np.float32)
    w1 = np.asarray(inputs["L0_ffn_w1"], dtype=np.float32)
    b1 = np.asarray(inputs["L0_ffn_b1"], dtype=np.float32)
    w2 = np.asarray(inputs["L0_ffn_w2"], dtype=np.float32)
    b2 = np.asarray(inputs["L0_ffn_b2"], dtype=np.float32)

    if "nc" not in _CACHE:
        _CACHE["nc"] = build_nc()
    nc, _ = _CACHE["nc"]

    pos = np.arange(T)
    diff = pos[:, None] - pos[None, :]
    expn = np.where(diff > 0, np.maximum(diff - 1, 0), 1.0e30).astype(np.float32)
    amat = expn.astype(ml_dtypes.bfloat16)
    ident = np.eye(128, dtype=np.float32)

    shared = {
        "w1": np.ascontiguousarray(w1.reshape(NE * 1024, 4096)),
        "w2": np.ascontiguousarray(w2.reshape(NE * 4096, 1024)),
        "basisT": np.ascontiguousarray(basis.T),
        "qcT": np.ascontiguousarray(qc.T),
        "kcT": np.ascontiguousarray(kc.T),
        "vcT": np.ascontiguousarray(vcc.T),
        "ocT": np.ascontiguousarray(oc.T),
        "wred": np.ascontiguousarray(wred),
        "wg": np.ascontiguousarray(wg),
        "b1t": np.ascontiguousarray(
            b1.reshape(NE, 32, 128).transpose(2, 0, 1).reshape(128, 256)),
        "b2t": np.ascontiguousarray(
            b2.reshape(NE, 8, 128).transpose(2, 0, 1).reshape(128, 64)),
        "dlog": np.ascontiguousarray(np.broadcast_to(dl.reshape(1, 1), (128, 1))),
        "oscale": np.ascontiguousarray(np.broadcast_to(osc.reshape(1, 1), (128, 1))),
        "ident": ident,
        "amat": amat,
    }
    in_maps = []
    for c in range(NCORE):
        m = dict(shared)
        m["x"] = np.ascontiguousarray(x[2 * c:2 * c + 2].reshape(NSEQ * T, 1024))
        in_maps.append(m)

    res = run_bass_kernel_spmd(nc, in_maps, core_ids=list(range(NCORE)))
    out = np.concatenate(
        [r["out"].reshape(NSEQ, T, 1024) for r in res.results], axis=0)
    return out


if __name__ == "__main__":
    import time
    t0 = time.time()
    nc, n_inst = build_nc()
    print("build s:", round(time.time() - t0, 1))
    print("instructions:", n_inst, "total:", sum(n_inst.values()))


# version 07
# speedup vs baseline: 1.1278x; 1.1278x over previous
import numpy as np
import ml_dtypes
import concourse.bass as bass
import concourse.mybir as mybir
from concourse.bass_utils import run_bass_kernel_spmd

F32 = mybir.dt.float32
BF16 = mybir.dt.bfloat16
AF = mybir.ActivationFunctionType
ALU = mybir.AluOpType
AX = mybir.AxisListType

P = 128
DC = 8
FC = 32
T = 2048
HT = 1024
NT = 16
NTH = 8
NE = 8
NSEQ = 2
NCORE = 8
TBL = 512
NTB = T // TBL


class B:
    ENGS = ("tensor", "vector", "scalar", "sync", "gpsimd")

    def __init__(self, nc):
        self.nc = nc
        self.plan = {k: [] for k in self.ENGS}
        self.sems = {}

    def sem(self, name):
        cm = self.nc.semaphore(name)
        h = cm.__enter__()
        self.sems[name] = [h, 0, cm]

    def close_sems(self):
        for name, (h, v, cm) in list(self.sems.items()):
            cm.__exit__(None, None, None)

    def v(self, name):
        return self.sems[name][1]

    def op(self, eng, fn, inc=None, n=1):
        if inc is not None:
            s = self.sems[inc]
            s[1] += n
            self.plan[eng].append(("o", fn, s[0], n))
            return s[1]
        self.plan[eng].append(("o", fn, None, 0))
        return None

    def wait(self, eng, sname, val):
        if val is None or val <= 0:
            return
        self.plan[eng].append(("w", self.sems[sname][0], val))

    def replay(self, eng, E):
        for it in self.plan[eng]:
            if it[0] == "w":
                E.wait_ge(it[1], it[2])
            else:
                r = it[1](E)
                if it[2] is not None:
                    r.then_inc(it[2], it[3])


def build_nc():
    nc = bass.Bass("TRN2", target_bir_lowering=False)

    x_in = nc.dram_tensor("x", [NSEQ * T, 1024], F32, kind="ExternalInput").ap()
    w1 = nc.dram_tensor("w1", [NE * 1024, 4096], F32, kind="ExternalInput").ap()
    w2 = nc.dram_tensor("w2", [NE * 4096, 1024], F32, kind="ExternalInput").ap()
    basisT_d = nc.dram_tensor("basisT", [256, 1024], F32, kind="ExternalInput").ap()
    coefT_d = [nc.dram_tensor(n, [256, 256], F32, kind="ExternalInput").ap()
               for n in ("qcT", "kcT", "vcT", "ocT")]
    wred_d = nc.dram_tensor("wred", [1024, 16], F32, kind="ExternalInput").ap()
    wg_d = nc.dram_tensor("wg", [8, 16], F32, kind="ExternalInput").ap()
    b1t_d = nc.dram_tensor("b1t", [128, 256], F32, kind="ExternalInput").ap()
    b2t_d = nc.dram_tensor("b2t", [128, 64], F32, kind="ExternalInput").ap()
    dlog_d = nc.dram_tensor("dlog", [128, 1], F32, kind="ExternalInput").ap()
    oscale_d = nc.dram_tensor("oscale", [128, 1], F32, kind="ExternalInput").ap()
    ident_d = nc.dram_tensor("ident", [128, 128], F32, kind="ExternalInput").ap()
    amat_d = nc.dram_tensor("amat", [T, T], BF16, kind="ExternalInput").ap()
    out_d = nc.dram_tensor("out", [NSEQ * T, 1024], F32, kind="ExternalOutput").ap()
    w1b = nc.dram_tensor("w1b", [NE * 1024, 4096], BF16).ap()
    w2b = nc.dram_tensor("w2b", [NE * 4096, 1024], BF16).ap()

    A = nc.alloc_sbuf_tensor
    x_sb = A("x_sb", [P, DC * T], F32).ap()
    xnT = A("xnT", [P, DC * T], BF16).ap()
    w1buf = [A(f"w1buf{i}", [P, DC * 512], BF16).ap() for i in range(2)]
    w2buf = [A(f"w2buf{i}", [P, 16 * 128], BF16).ap() for i in range(2)]
    xstage = [A(f"xstage{i}", [P, 1024], F32).ap() for i in range(2)]
    for i in range(2):
        w2buf.append(nc.alloc_sbuf_tensor_at(
            f"w2buf{2 + i}", [P, 16 * 128], BF16,
            offset=nc.lookup_mloc(xstage[i].tensor).addr).ap())
    tmpv = A("tmpv", [P, 1024], F32).ap()
    coef_rep = A("coef_rep", [P, HT], F32).ap()
    coefcols = A("coefcols", [P, 2 * NTH * NE], F32).ap()
    b1t = A("b1t_sb", [P, 256], F32).ap()
    b2t = A("b2t_sb", [P, 64], F32).ap()
    id32 = A("id32", [P, 128], F32).ap()
    identbf = A("identbf", [P, 128], BF16).ap()
    onesb = A("onesb", [P, 128], BF16).ap()
    ones32 = A("ones32", [P, 1], F32).ap()
    wred_sb = A("wred_sb", [P, DC * 16], BF16).ap()
    wg_sb = A("wg_sb", [P, 16], F32).ap()
    wgsq = A("wgsq", [P, 16], F32).ap()
    wgn = A("wgn", [P, 16], F32).ap()
    wgnT = A("wgnT", [P, 8], F32).ap()
    nrm = A("nrm", [P, 2], F32).ap()
    diagb = [A(f"diagb{i}", [P, 128], BF16).ap() for i in range(2)]
    dlog_sb = A("dlog_sb", [P, 1], F32).ap()
    sigd = A("sigd", [P, 1], F32).ap()
    logd = A("logd", [P, 1], F32).ap()
    oscale_sb = A("oscale_sb", [P, 1], F32).ap()
    ceps = A("ceps", [P, 1], F32).ap()
    gexp = A("gexp", [P, 8], F32).ap()
    negm = A("negm", [P, 1], F32).ap()
    gsum = A("gsum", [P, 1], F32).ap()
    rs = A("rs", [P, 1], F32).ap()
    is8 = A("is8", [P, 8], F32).ap()

    AR = (nc.sbuf_bytes_remaining - 64) // 32 * 32
    assert AR >= 66560, AR
    arena = A("arena", [P, AR], mybir.dt.uint8)
    base = nc.lookup_mloc(arena).addr

    def at(name, shape, dtype, off):
        nb = int(np.prod(shape[1:])) * mybir.dt.size(dtype)
        assert off + nb <= AR, (name, off, nb)
        return nc.alloc_sbuf_tensor_at(name, shape, dtype, offset=base + off).ap()

    hT = at("hT", [P, FC * HT], BF16, 0)
    xsq = at("xsq", [P, DC * HT], BF16, 0)
    mrow = at("mrow", [1, HT], F32, 16384)
    srow = at("srow", [1, HT], F32, 20480)
    rrow = at("rrow", [1, HT], F32, 24576)
    mbf = at("mbf", [1, HT], BF16, 28672)
    rbf = at("rbf", [1, HT], BF16, 30720)
    redT = at("redT", [P, HT], F32, 32768)
    QT = at("QT", [P, 2 * T], BF16, 0)
    KT = at("KT", [P, 2 * T], BF16, 8192)
    V_sb = at("V_sb", [P, NT * 256], BF16, 16384)
    SdT = at("SdT", [P, NT * TBL], BF16, 24576)
    rT = at("rT", [P, 2 * TBL], BF16, 40960)
    projbuf = at("projbuf", [P, DC * 256], BF16, 43008)
    owT = at("owT", [P, 2 * 1024], BF16, 43008)
    bsT = at("bsT", [P, 2 * 1024], BF16, 47104)
    cT4 = at("cT4", [P, 4 * 512], BF16, 51200)
    abuf = [at(f"abuf{i}", [P, TBL], BF16, 55296 + i * 1024) for i in range(2)]
    mexp = [at(f"mexp{i}", [P, TBL], F32, 57344 + i * 2048) for i in range(2)]
    mexpd = [at(f"mexpd{i}", [P, TBL], F32, 61440 + i * 2048) for i in range(2)]

    Pb = [nc.alloc_psum_tensor(f"P{i}", [P, 512], F32).ap() for i in range(4)]
    Yb = [nc.alloc_psum_tensor(f"Y{i}", [P, 1024], F32).ap() for i in range(2)]

    b = B(nc)
    for s in ("SP", "SV", "SC", "SX0", "SX1", "SO0", "SO1", "SG", "SA20", "SA21", "SCV", "SW10", "SW11", "SW20", "SW21", "SW22", "SW23"):
        b.sem(s)
    SW1 = ("SW10", "SW11")
    SW2 = ("SW20", "SW21", "SW22", "SW23")

    free_at = {}

    def reuse_wait(eng, *keys):
        for k in keys:
            ev = free_at.get(k)
            if ev:
                b.wait(eng, ev[0], ev[1])

    def set_free(key, sname, val):
        free_at[key] = (sname, val)


    def small(fn):
        v = b.op("gpsimd", fn, "SG", 16)
        b.wait("gpsimd", "SG", v)
        return v

    small(lambda E: E.dma_start(identbf[:, :], ident_d[:, :]))
    small(lambda E: E.dma_start(
        wred_sb[:, :], wred_d.rearrange("(dc p) g -> p dc g", p=P)))
    small(lambda E: E.dma_start(b1t[:, :], b1t_d[:, :]))
    small(lambda E: E.dma_start(b2t[:, :], b2t_d[:, :]))
    small(lambda E: E.dma_start(dlog_sb[:, :], dlog_d[:, :]))
    small(lambda E: E.dma_start(oscale_sb[:, :], oscale_d[:, :]))
    sg_smalls = small(lambda E: E.dma_start(wg_sb[0:8, :], wg_d[:, :]))

    sg_id = small(lambda E: E.dma_start(id32[:, :], ident_d[:, :]))

    cast_idx = {}
    for e in range(NE):
        for sl in range(8):
            cast_idx[("w1", e, sl)] = b.op("gpsimd", lambda E, e=e, sl=sl: E.dma_start(
                w1b[e * 1024:(e + 1) * 1024, sl * 512:(sl + 1) * 512],
                w1[e * 1024:(e + 1) * 1024, sl * 512:(sl + 1) * 512]), "SCV", 16)
            b.wait("gpsimd", "SCV", b.v("SCV"))
        for ck in range(8):
            cast_idx[("w2", e, ck)] = b.op("gpsimd", lambda E, e=e, ck=ck: E.dma_start(
                w2b[e * 4096 + ck * 512:e * 4096 + (ck + 1) * 512, :],
                w2[e * 4096 + ck * 512:e * 4096 + (ck + 1) * 512, :]), "SCV", 16)
            b.wait("gpsimd", "SCV", b.v("SCV"))


    b.op("vector", lambda E: E.memset(onesb[:, :], 1.0), "SV")
    b.op("vector", lambda E: E.memset(ones32[:, :], 1.0), "SV")
    b.op("vector", lambda E: E.memset(ceps[:, :], 1e-5), "SV")
    sv_memset = b.v("SV")

    b.wait("scalar", "SG", sg_smalls)
    sc1 = b.op("scalar", lambda E: E.activation(sigd[:, :], dlog_sb[:, :], AF.Sigmoid), "SC")
    b.wait("scalar", "SC", sc1)
    sc2 = b.op("scalar", lambda E: E.activation(logd[:, :], sigd[:, :], AF.Ln), "SC")
    b.wait("vector", "SC", sc2)
    sv_logd = b.op("vector", lambda E: E.tensor_scalar_min(logd[:, :], logd[:, :], -1e-7), "SV")

    b.wait("scalar", "SG", sg_smalls)
    sc3 = b.op("scalar", lambda E: E.square(wgsq[0:8, :], wg_sb[0:8, :]), "SC")
    b.wait("vector", "SC", sc3)
    sv1 = b.op("vector", lambda E: E.reduce_sum(nrm[0:8, 0:1], wgsq[0:8, :], axis=AX.X), "SV")
    b.wait("scalar", "SV", sv1)
    sc4 = b.op("scalar", lambda E: E.activation(nrm[0:8, 1:2], nrm[0:8, 0:1], AF.Sqrt), "SC")
    b.wait("vector", "SC", sc4)
    sv2 = b.op("vector", lambda E: E.tensor_scalar_max(nrm[0:8, 1:2], nrm[0:8, 1:2], 1e-4), "SV")
    b.wait("vector", "SV", sv2)
    sv3 = b.op("vector", lambda E: E.reciprocal(nrm[0:8, 0:1], nrm[0:8, 1:2]), "SV")
    b.wait("vector", "SV", sv3)
    sv4 = b.op("vector", lambda E: E.tensor_scalar(
        wgn[0:8, :], wg_sb[0:8, :], nrm[0:8, 0:1], None, ALU.mult), "SV")
    b.wait("tensor", "SV", sv4)
    b.wait("tensor", "SG", sg_id)
    sp1 = b.op("tensor", lambda E: E.transpose(Pb[0][0:16, 0:8], wgn[0:8, 0:16], id32[0:8, 0:8]), "SP")
    b.wait("vector", "SP", sp1)
    sv5 = b.op("vector", lambda E: E.tensor_copy(wgnT[0:16, 0:8], Pb[0][0:16, 0:8]), "SV")
    set_free("P0", "SV", sv5)

    x3 = x_sb.rearrange("p (dc t) -> p dc t", t=T)

    def load_seq(seq):
        sc_last = None
        for tile in range(NT):
            par = tile % 2
            reuse_wait("sync", f"xstage{par}")
            sxn = f"SX{par}"
            b.op("sync", lambda E, seq=seq, tile=tile, par=par: E.dma_start(
                xstage[par][:, :],
                x_in[seq * T + tile * 128:seq * T + (tile + 1) * 128, :]), sxn, 16)
            b.wait("tensor", sxn, b.v(sxn))
            if tile == 0:
                b.wait("tensor", "SG", sg_id)
            reuse_wait("tensor", "P0", "P1")
            sp = None
            for dc in range(DC):
                pp = Pb[0] if dc < 4 else Pb[1]
                sp = b.op("tensor", lambda E, dc=dc, par=par, pp=pp: E.transpose(
                    pp[:, (dc % 4) * 128:(dc % 4 + 1) * 128],
                    xstage[par][:, dc * 128:(dc + 1) * 128], id32[:, :]), "SP")
            set_free(f"xstage{par}", "SP", sp)
            b.wait("scalar", "SP", sp)
            for g in range(2):
                sc_last = b.op("scalar", lambda E, g=g, tile=tile: E.copy(
                    x3[:, g * 4:(g + 1) * 4, tile * 128:(tile + 1) * 128],
                    Pb[g][:, 0:512].rearrange("p (a t) -> p a t", t=128)), "SC")
            set_free("P0", "SC", sc_last)
            set_free("P1", "SC", sc_last)
        spend = b.v("SP")
        set_free("w2buf2", "SP", spend)
        set_free("w2buf3", "SP", spend)
        return ("SC", sc_last)

    def ln_half(seq, subl, half, routing, ev_x):
        t0 = half * HT
        b.wait("scalar", "SP", b.v("SP"))
        b.wait("scalar", ev_x[0], ev_x[1])
        b.wait("tensor", ev_x[0], ev_x[1])
        sc_sq = None
        for dc in range(DC):
            sc_sq = b.op("scalar", lambda E, dc=dc, t0=t0: E.square(
                xsq[:, dc * HT:(dc + 1) * HT], x_sb[:, dc * T + t0:dc * T + t0 + HT]), "SC")
        reuse_wait("tensor", "Y0", "Y1")
        b.wait("tensor", "SV", sv_memset)
        for dc in range(DC):
            for g in range(2):
                b.op("tensor", lambda E, dc=dc, g=g, t0=t0: E.matmul(
                    Yb[0][0:1, g * 512:(g + 1) * 512], ones32[:, 0:1],
                    x_sb[:, dc * T + t0 + g * 512:dc * T + t0 + (g + 1) * 512],
                    start=(dc == 0), stop=(dc == DC - 1)), "SP")
        b.wait("tensor", "SC", sc_sq)
        sp_st = None
        for dc in range(DC):
            for g in range(2):
                sp_st = b.op("tensor", lambda E, dc=dc, g=g: E.matmul(
                    Yb[1][0:1, g * 512:(g + 1) * 512], onesb[:, 0:1],
                    xsq[:, dc * HT + g * 512:dc * HT + (g + 1) * 512],
                    start=(dc == 0), stop=(dc == DC - 1)), "SP")
        b.wait("vector", "SP", sp_st)
        sv_m = b.op("vector", lambda E: E.tensor_scalar(
            mrow[0:1, :], Yb[0][0:1, 0:HT], 1.0 / 1024.0, None, ALU.mult), "SV")
        b.wait("scalar", "SV", sv_m)
        sc_msq = b.op("scalar", lambda E: E.square(srow[0:1, :], mrow[0:1, :]), "SC")
        b.wait("vector", "SC", sc_msq)
        sv_var = b.op("vector", lambda E: E.scalar_tensor_tensor(
            rrow[0:1, :], Yb[1][0:1, 0:HT], 1.0 / 1024.0, srow[0:1, :],
            ALU.mult, ALU.subtract), "SV")
        b.wait("scalar", "SV", sv_var)
        sc_sd = b.op("scalar", lambda E: E.activation(
            srow[0:1, :], rrow[0:1, :], AF.Sqrt, bias=ceps[0:1, :]), "SC")
        b.wait("vector", "SC", sc_sd)
        sv_r = b.op("vector", lambda E: E.reciprocal(rrow[0:1, :], srow[0:1, :]), "SV")
        b.wait("vector", "SV", sv_r)
        sv_mb = b.op("vector", lambda E: E.tensor_copy(mbf[0:1, :], mrow[0:1, :]), "SV")
        sv_rb = b.op("vector", lambda E: E.tensor_copy(rbf[0:1, :], rrow[0:1, :]), "SV")
        b.wait("tensor", "SV", sv_rb)
        for g in range(2):
            b.op("tensor", lambda E, g=g: E.matmul(
                Yb[0][:, g * 512:(g + 1) * 512], onesb[0:1, 0:128],
                mbf[0:1, g * 512:(g + 1) * 512], start=True, stop=True), "SP")
        sp_rep = None
        for g in range(2):
            sp_rep = b.op("tensor", lambda E, g=g: E.matmul(
                Yb[1][:, g * 512:(g + 1) * 512], onesb[0:1, 0:128],
                rbf[0:1, g * 512:(g + 1) * 512], start=True, stop=True), "SP")
        b.wait("vector", "SP", sp_rep)
        sv_xn = None
        for dc in range(DC):
            reuse_wait("vector", "tmpv")
            sv_t = b.op("vector", lambda E, dc=dc, t0=t0: E.tensor_tensor(
                tmpv[:, 0:HT], x_sb[:, dc * T + t0:dc * T + t0 + HT],
                Yb[0][:, 0:HT], ALU.subtract), "SV")
            b.wait("vector", "SV", sv_t)
            sv_xn = b.op("vector", lambda E, dc=dc, t0=t0: E.tensor_tensor(
                xnT[:, dc * T + t0:dc * T + t0 + HT], tmpv[:, 0:HT],
                Yb[1][:, 0:HT], ALU.mult), "SV")
            set_free("tmpv", "SV", sv_xn)
        set_free("Y0", "SV", sv_xn)
        set_free("Y1", "SV", sv_xn)
        if not routing:
            return ("SV", sv_xn)

        reuse_wait("tensor", "P0", "P1")
        b.wait("tensor", "SV", sv_xn)
        sp_red = None
        for dc in range(DC):
            for g in range(2):
                sp_red = b.op("tensor", lambda E, dc=dc, g=g, t0=t0: E.matmul(
                    Pb[g][0:16, 0:512], wred_sb[:, dc * 16:(dc + 1) * 16],
                    xnT[:, dc * T + t0 + g * 512:dc * T + t0 + (g + 1) * 512],
                    start=(dc == 0), stop=(dc == DC - 1)), "SP")
        b.wait("vector", "SP", sp_red)
        sv_red = None
        for g in range(2):
            sv_red = b.op("vector", lambda E, g=g: E.tensor_copy(
                redT[0:16, g * 512:(g + 1) * 512], Pb[g][0:16, 0:512]), "SV")
        set_free("P0", "SV", sv_red)
        set_free("P1", "SV", sv_red)
        co = half * NTH * NE
        sc_ex_prev = None
        sv_cf_prev = None
        sv_cf = None
        for tile in range(NTH):
            key = f"P{2 + tile % 2}"
            pp = Pb[2 + tile % 2]
            reuse_wait("tensor", key)
            b.wait("tensor", "SV", sv_red)
            sp_lg = b.op("tensor", lambda E, tile=tile, pp=pp: E.matmul(
                pp[:, tile * 8:tile * 8 + 8], redT[0:16, tile * 128:(tile + 1) * 128],
                wgnT[0:16, 0:8], start=True, stop=True), "SP")
            b.wait("vector", "SP", sp_lg)
            if sc_ex_prev:
                b.wait("vector", "SC", sc_ex_prev)
            sv_nm = b.op("vector", lambda E, tile=tile, pp=pp: E.reduce_max(
                negm[:, 0:1], pp[:, tile * 8:tile * 8 + 8], axis=AX.X, negate=True), "SV")
            b.wait("scalar", "SV", sv_nm)
            sc_ex = b.op("scalar", lambda E, tile=tile, pp=pp: E.activation(
                gexp[:, 0:8], pp[:, tile * 8:tile * 8 + 8], AF.Exp, bias=negm[:, 0:1]), "SC")
            sc_ex_prev = sc_ex
            b.wait("vector", "SC", sc_ex)
            sv_sm = b.op("vector", lambda E: E.reduce_sum(
                gsum[:, 0:1], gexp[:, 0:8], axis=AX.X), "SV")
            b.wait("vector", "SV", sv_sm)
            if sv_cf_prev:
                b.wait("vector", "SV", sv_cf_prev)
            sv_rs = b.op("vector", lambda E: E.reciprocal(rs[:, 0:1], gsum[:, 0:1]), "SV")
            sv_is = b.op("vector", lambda E, tile=tile, pp=pp: E.tensor_scalar(
                is8[:, 0:8], pp[:, tile * 8:tile * 8 + 8], negm[:, 0:1], 0.0,
                ALU.add, ALU.is_ge), "SV")
            b.wait("vector", "SV", sv_is)
            sv_cf = b.op("vector", lambda E, tile=tile, co=co: E.tensor_scalar(
                coefcols[:, co + tile * 8:co + tile * 8 + 8], is8[:, 0:8],
                rs[:, 0:1], 0.5, ALU.mult, ALU.mult), "SV")
            sv_cf_prev = sv_cf
            set_free(key, "SV", sv_cf)
        return ("SV", sv_cf)

    wdma = {"w1n": [0, 0], "w2n": [0, 0, 0, 0],
            "w1free": [None, None], "w2free": [None, None, None, None]}

    def issue_w1(e, sl, first_use):
        par = sl % 2
        ev = wdma["w1free"][par]
        if ev:
            b.wait("sync", ev[0], ev[1])
        if first_use:
            b.wait("sync", "SCV", cast_idx[("w1", e, sl)])
        wdma["w1n"][par] += 1
        b.op("sync", lambda E, e=e, sl=sl, par=par: E.dma_start(
            w1buf[par][:, :],
            w1b[e * 1024:(e + 1) * 1024, sl * 512:(sl + 1) * 512]
            .rearrange("(dc p) f -> p dc f", p=P)), SW1[par], 16)
        return wdma["w1n"][par]

    def issue_w2(e, vc, ch, first_use):
        par = (vc * 2 + ch) % 2
        ev = wdma["w2free"][par]
        if ev:
            b.wait("sync", ev[0], ev[1])
        if first_use:
            b.wait("sync", "SCV", cast_idx[("w2", e, ch * 4 + 3)])
        wdma["w2n"][par] += 1
        if par >= 2:
            reuse_wait("sync", f"w2buf{par}")
        b.op("sync", lambda E, e=e, vc=vc, ch=ch, par=par: E.dma_start(
            w2buf[par][:, :],
            w2b[e * 4096 + ch * 2048:e * 4096 + (ch + 1) * 2048,
                vc * 128:(vc + 1) * 128]
            .rearrange("(cl p) v -> p cl v", p=P)), SW2[par], 16)
        return wdma["w2n"][par]

    def ffn_half(seq, subl, half, first_use):
        t0 = half * HT
        co = half * NTH * NE
        sv_last = b.v("SV")
        for e in range(NE):
            fu = (seq == 0)
            w1rdy = {}
            w1rdy[0] = issue_w1(e, 0, fu)
            w1rdy[1] = issue_w1(e, 1, fu)
            w2rdy = {}
            for (pv, pc) in ((0, 0), (0, 1)):
                w2rdy[(pv, pc)] = issue_w2(e, pv, pc, fu)

            sp_cf_last = None
            for c in range(FC):
                sl, cl, spar = c // 4, c % 4, (c // 4) % 2
                bk = (Pb[0], Pb[1]) if c % 2 == 0 else (Pb[2], Pb[3])
                bkeys = ("P0", "P1") if c % 2 == 0 else ("P2", "P3")
                if cl == 0:
                    b.wait("tensor", SW1[spar], 16 * w1rdy[sl])
                reuse_wait("tensor", *bkeys)
                sp_h = None
                for k in range(DC):
                    for tg in range(2):
                        sp_h = b.op("tensor", lambda E, k=k, tg=tg, cl=cl, spar=spar, bk=bk, t0=t0: E.matmul(
                            bk[tg][:, 0:512],
                            w1buf[spar][:, k * 512 + cl * 128:k * 512 + (cl + 1) * 128],
                            xnT[:, k * T + t0 + tg * 512:k * T + t0 + (tg + 1) * 512],
                            start=(k == 0), stop=(k == DC - 1)), "SP")
                if cl == 3:
                    wdma["w1free"][sl % 2] = ("SP", sp_h)
                    if sl + 2 < 8:
                        w1rdy[sl + 2] = issue_w1(e, sl + 2, fu)
                if c == 8:
                    reuse_wait("tensor", "Y0")
                if 8 <= c < 16:
                    tile = c - 8
                    par = tile % 2
                    reuse_wait("vector", f"diag{par}")
                    sv_d = b.op("vector", lambda E, tile=tile, par=par, e=e, co=co: E.tensor_scalar(
                        diagb[par][:, :], identbf[:, :],
                        coefcols[:, co + tile * 8 + e:co + tile * 8 + e + 1], None, ALU.mult), "SV")
                    b.wait("tensor", "SV", sv_d)
                    sp_cf_last = b.op("tensor", lambda E, tile=tile, par=par: E.matmul(
                        Yb[0][:, tile * 128:(tile + 1) * 128], onesb[:, 0:128],
                        diagb[par][:, :], start=True, stop=True), "SP")
                    set_free(f"diag{par}", "SP", sp_cf_last)
                b.wait("scalar", "SP", sp_h)
                sc_g = None
                for tg in range(2):
                    sc_g = b.op("scalar", lambda E, c=c, tg=tg, e=e, bk=bk: E.activation(
                        hT[:, c * HT + tg * 512:c * HT + (tg + 1) * 512], bk[tg][:, 0:512],
                        AF.Gelu, bias=b1t[:, e * 32 + c:e * 32 + c + 1]), "SC")
                set_free(bkeys[0], "SC", sc_g)
                set_free(bkeys[1], "SC", sc_g)
            sc_h_all = b.v("SC")
            reuse_wait("vector", "coef_rep")
            b.wait("vector", "SP", sp_cf_last)
            sv_coef = b.op("vector", lambda E: E.tensor_copy(
                coef_rep[:, 0:1024], Yb[0][:, 0:1024]), "SV")
            set_free("Y0", "SV", sv_coef)
            b.wait("tensor", "SC", sc_h_all)

            sv_x = None
            for vc in range(NE):
                yb = Yb[vc % 2]
                ykey = f"Y{vc % 2}"
                reuse_wait("tensor", ykey)
                sp_y = None
                for ch in range(2):
                    par2 = (vc * 2 + ch) % 2
                    b.wait("tensor", SW2[par2], 16 * w2rdy[(vc, ch)])
                    for cl in range(16):
                        c = ch * 16 + cl
                        for tg in range(2):
                            sp_y = b.op("tensor", lambda E, cl=cl, c=c, tg=tg, par2=par2, yb=yb: E.matmul(
                                yb[:, tg * 512:(tg + 1) * 512],
                                w2buf[par2][:, cl * 128:(cl + 1) * 128],
                                hT[:, c * HT + tg * 512:c * HT + (tg + 1) * 512],
                                start=(c == 0), stop=(c == FC - 1)), "SP")
                    wdma["w2free"][par2] = ("SP", sp_y)
                    if vc + 1 < NE:
                        w2rdy[(vc + 1, ch)] = issue_w2(e, vc + 1, ch, fu)
                b.wait("vector", "SP", sp_y)
                b.wait("vector", "SV", sv_coef)
                reuse_wait("vector", "tmpv")
                sv_a = b.op("vector", lambda E, vc=vc, e=e, yb=yb: E.tensor_scalar(
                    tmpv[:, 0:HT], yb[:, 0:HT], b2t[:, e * 8 + vc:e * 8 + vc + 1],
                    None, ALU.add), "SV")
                set_free(ykey, "SV", sv_a)
                b.wait("vector", "SV", sv_a)
                sv_b = b.op("vector", lambda E: E.tensor_tensor(
                    tmpv[:, 0:HT], tmpv[:, 0:HT], coef_rep[:, 0:HT], ALU.mult), "SV")
                b.wait("vector", "SV", sv_b)
                sv_x = b.op("vector", lambda E, vc=vc, t0=t0: E.tensor_tensor(
                    x_sb[:, vc * T + t0:vc * T + t0 + HT],
                    x_sb[:, vc * T + t0:vc * T + t0 + HT], tmpv[:, 0:HT], ALU.add), "SV")
                set_free("tmpv", "SV", sv_x)
        spend = b.v("SP")
        set_free("xstage0", "SP", spend)
        set_free("xstage1", "SP", spend)
        return ("SV", sv_x)

    def attention(seq):
        spbar = b.v("SP")
        b.wait("gpsimd", "SP", spbar)
        b.wait("scalar", "SP", spbar)
        b.wait("vector", "SP", spbar)
        v = b.op("gpsimd", lambda E: E.dma_start(
            bsT[:, :], basisT_d.rearrange("(nb p) v -> p nb v", p=P)), "SG", 16)
        b.wait("gpsimd", "SG", v)
        for i in range(4):
            v = b.op("gpsimd", lambda E, i=i: E.dma_start(
                cT4[:, i * 512:(i + 1) * 512],
                coefT_d[i].rearrange("(nb p) c -> p nb c", p=P)), "SG", 16)
            b.wait("gpsimd", "SG", v)
        sg_attn = b.v("SG")

        def build_proj(which):
            b.wait("tensor", "SG", sg_attn)
            sc = None
            for dc in range(DC):
                key = f"P{dc % 2}"
                pp = Pb[dc % 2]
                reuse_wait("tensor", key)
                sp = None
                for nb in range(2):
                    sp = b.op("tensor", lambda E, dc=dc, nb=nb, pp=pp, w=which: E.matmul(
                        pp[:, 0:256], bsT[:, nb * 1024 + dc * 128:nb * 1024 + (dc + 1) * 128],
                        cT4[:, w * 512 + nb * 256:w * 512 + (nb + 1) * 256],
                        start=(nb == 0), stop=(nb == 1)), "SP")
                b.wait("scalar", "SP", sp)
                sc = b.op("scalar", lambda E, dc=dc, pp=pp: E.copy(
                    projbuf[:, dc * 256:(dc + 1) * 256], pp[:, 0:256]), "SC")
                set_free(key, "SC", sc)
            return sc

        def build_qkt(dst, ev_proj):
            b.wait("tensor", "SC", ev_proj)
            sc = None
            for cc in range(2):
                reuse_wait("tensor", "Y0", "Y1")
                sp = None
                for dc in range(DC):
                    for tg in range(4):
                        yb = Yb[tg // 2]
                        sp = b.op("tensor", lambda E, cc=cc, dc=dc, tg=tg, yb=yb: E.matmul(
                            yb[:, (tg % 2) * 512:(tg % 2 + 1) * 512],
                            projbuf[:, dc * 256 + cc * 128:dc * 256 + (cc + 1) * 128],
                            xnT[:, dc * T + tg * 512:dc * T + (tg + 1) * 512],
                            start=(dc == 0), stop=(dc == DC - 1)), "SP")
                b.wait("scalar", "SP", sp)
                for g in range(2):
                    sc = b.op("scalar", lambda E, cc=cc, g=g, dst=dst: E.copy(
                        dst[:, cc * T + g * 1024:cc * T + (g + 1) * 1024],
                        Yb[g][:, 0:1024]), "SC")
                set_free("Y0", "SC", sc)
                set_free("Y1", "SC", sc)
            return sc

        ev = build_proj(0)
        sc_qt = build_qkt(QT, ev)
        ev = build_proj(1)
        sc_kt = build_qkt(KT, ev)
        ev_v = build_proj(2)
        b.wait("tensor", "SC", ev_v)
        sc_v = None
        for st in range(NT):
            key = f"P{2 + st % 2}"
            pp = Pb[2 + st % 2]
            reuse_wait("tensor", key)
            sp = None
            for dc in range(DC):
                sp = b.op("tensor", lambda E, st=st, dc=dc, pp=pp: E.matmul(
                    pp[:, 0:256], xnT[:, dc * T + st * 128:dc * T + (st + 1) * 128],
                    projbuf[:, dc * 256:(dc + 1) * 256],
                    start=(dc == 0), stop=(dc == DC - 1)), "SP")
            b.wait("scalar", "SP", sp)
            sc_v = b.op("scalar", lambda E, st=st, pp=pp: E.copy(
                V_sb[:, st * 256:(st + 1) * 256], pp[:, 0:256]), "SC")
            set_free(key, "SC", sc_v)
        sc_ow = None
        for cc in range(2):
            reuse_wait("tensor", "Y0")
            b.wait("tensor", "SC", sc_v)
            sp = None
            for nb in range(2):
                for g in range(2):
                    sp = b.op("tensor", lambda E, cc=cc, nb=nb, g=g: E.matmul(
                        Yb[0][:, g * 512:(g + 1) * 512],
                        cT4[:, 3 * 512 + nb * 256 + cc * 128:3 * 512 + nb * 256 + (cc + 1) * 128],
                        bsT[:, nb * 1024 + g * 512:nb * 1024 + (g + 1) * 512],
                        start=(nb == 0), stop=(nb == 1)), "SP")
            b.wait("scalar", "SP", sp)
            sc_ow = b.op("scalar", lambda E, cc=cc: E.copy(
                owT[:, cc * 1024:(cc + 1) * 1024], Yb[0][:, 0:1024]), "SC")
            set_free("Y0", "SC", sc_ow)

        sv_o = None
        for tb in range(NTB):
            sd_done = {}
            for st in range(NT):
                par = st % 2
                reuse_wait("scalar", f"abuf{par}")
                san = f"SA2{par}"
                b.op("scalar", lambda E, st=st, tb=tb, par=par: E.dma_start(
                    abuf[par][:, :],
                    amat_d[st * 128:(st + 1) * 128, tb * TBL:(tb + 1) * TBL]), san, 16)
                b.wait("vector", san, b.v(san))
                b.wait("vector", "SV", sv_logd)
                reuse_wait("vector", f"mexp{par}")
                sv_m1 = b.op("vector", lambda E, par=par: E.tensor_scalar(
                    mexp[par][:, :], abuf[par][:, :], logd[:, 0:1], None, ALU.mult), "SV")
                set_free(f"abuf{par}", "SV", sv_m1)
                b.wait("scalar", "SV", sv_m1)
                reuse_wait("scalar", f"mexpd{par}")
                sc_m2 = b.op("scalar", lambda E, par=par: E.activation(
                    mexpd[par][:, :], mexp[par][:, :], AF.Exp), "SC")
                set_free(f"mexp{par}", "SC", sc_m2)
                key = f"P{st % 2}"
                pp = Pb[st % 2]
                reuse_wait("tensor", key)
                if st == 0 and tb == 0:
                    b.wait("tensor", "SC", sc_kt)
                sp_s = None
                for cc in range(2):
                    sp_s = b.op("tensor", lambda E, st=st, tb=tb, cc=cc, pp=pp: E.matmul(
                        pp[:, 0:TBL], KT[:, cc * T + st * 128:cc * T + (st + 1) * 128],
                        QT[:, cc * T + tb * TBL:cc * T + (tb + 1) * TBL],
                        start=(cc == 0), stop=(cc == 1)), "SP")
                b.wait("vector", "SC", sc_m2)
                b.wait("vector", "SP", sp_s)
                sv_sd = b.op("vector", lambda E, st=st, par=par, pp=pp: E.tensor_tensor(
                    SdT[:, st * TBL:(st + 1) * TBL], pp[:, 0:TBL], mexpd[par][:, :],
                    ALU.mult), "SV")
                set_free(key, "SV", sv_sd)
                set_free(f"mexpd{par}", "SV", sv_sd)
                sd_done[st] = sv_sd
            sc_rt = None
            for cc in range(2):
                key = f"P{2 + cc}"
                pp = Pb[2 + cc]
                reuse_wait("tensor", key)
                sp_r = None
                for st in range(NT):
                    b.wait("tensor", "SV", sd_done[st])
                    sp_r = b.op("tensor", lambda E, cc=cc, st=st, pp=pp: E.matmul(
                        pp[:, 0:TBL], V_sb[:, st * 256 + cc * 128:st * 256 + (cc + 1) * 128],
                        SdT[:, st * TBL:(st + 1) * TBL],
                        start=(st == 0), stop=(st == NT - 1)), "SP")
                b.wait("scalar", "SP", sp_r)
                sc_rt = b.op("scalar", lambda E, cc=cc, pp=pp: E.copy(
                    rT[:, cc * TBL:(cc + 1) * TBL], pp[:, 0:TBL]), "SC")
                set_free(key, "SC", sc_rt)
            for vcq in range(NE):
                ykey = f"Y{vcq % 2}"
                yb = Yb[vcq % 2]
                reuse_wait("tensor", ykey)
                b.wait("tensor", "SC", sc_rt)
                sp_o = None
                for cc in range(2):
                    sp_o = b.op("tensor", lambda E, vcq=vcq, cc=cc, yb=yb: E.matmul(
                        yb[:, 0:TBL], owT[:, cc * 1024 + vcq * 128:cc * 1024 + (vcq + 1) * 128],
                        rT[:, cc * TBL:(cc + 1) * TBL],
                        start=(cc == 0), stop=(cc == 1)), "SP")
                b.wait("vector", "SP", sp_o)
                sv_o = b.op("vector", lambda E, vcq=vcq, tb=tb, yb=yb: E.scalar_tensor_tensor(
                    x_sb[:, vcq * T + tb * TBL:vcq * T + (tb + 1) * TBL], yb[:, 0:TBL],
                    oscale_sb[:, 0:1],
                    x_sb[:, vcq * T + tb * TBL:vcq * T + (tb + 1) * TBL],
                    ALU.mult, ALU.add), "SV")
                set_free(ykey, "SV", sv_o)
        return ("SV", sv_o)

    def store_seq(seq, ev_final):
        b.wait("tensor", ev_final[0], ev_final[1])
        for tile in range(NT):
            par = tile % 2
            reuse_wait("tensor", "P0", "P1")
            sp = None
            for dc in range(DC):
                pp = Pb[0] if dc < 4 else Pb[1]
                sp = b.op("tensor", lambda E, dc=dc, tile=tile, pp=pp: E.transpose(
                    pp[:, (dc % 4) * 128:(dc % 4 + 1) * 128],
                    x_sb[:, dc * T + tile * 128:dc * T + (tile + 1) * 128], id32[:, :]), "SP")
            b.wait("scalar", "SP", sp)
            reuse_wait("scalar", f"xstage{par}")
            sc = None
            for g in range(2):
                sc = b.op("scalar", lambda E, g=g, par=par: E.copy(
                    xstage[par][:, g * 512:(g + 1) * 512], Pb[g][:, 0:512]), "SC")
            set_free("P0", "SC", sc)
            set_free("P1", "SC", sc)
            b.wait("gpsimd", "SC", sc)
            son = f"SO{par}"
            b.op("gpsimd", lambda E, seq=seq, tile=tile, par=par: E.dma_start(
                out_d[seq * T + tile * 128:seq * T + (tile + 1) * 128, :],
                xstage[par][:, :]), son, 16)
            set_free(f"xstage{par}", son, b.v(son))
            set_free(f"w2buf{2 + par}", son, b.v(son))

    for seq in range(NSEQ):
        ev = load_seq(seq)
        for subl in range(3):
            if subl != 1:
                ev_r = {}
                for half in range(2):
                    ev_r[half] = ln_half(seq, subl, half, True, ev)
                for half in range(2):
                    ev = ffn_half(seq, subl, half,
                                  first_use=(seq == 0 and subl == 0))
            else:
                for half in range(2):
                    ev = ln_half(seq, subl, half, False, ev)
                ev = attention(seq)
        store_seq(seq, ev)

    b.wait("gpsimd", "SO0", b.v("SO0"))
    b.wait("gpsimd", "SO1", b.v("SO1"))
    b.wait("sync", "SO0", b.v("SO0"))
    b.wait("sync", "SO1", b.v("SO1"))

    with nc.Block() as block:
        @block.tensor
        def _(E):
            b.replay("tensor", E)

        @block.vector
        def _(E):
            b.replay("vector", E)

        @block.scalar
        def _(E):
            b.replay("scalar", E)

        @block.sync
        def _(E):
            b.replay("sync", E)

        @block.gpsimd
        def _(E):
            b.replay("gpsimd", E)

    b.close_sems()
    n_inst = {k: len(b.plan[k]) for k in b.plan}
    return nc, n_inst


_CACHE = {}


def kernel(**inputs):
    x = np.asarray(inputs["L0_attn_x"], dtype=np.float32)
    basis = np.asarray(inputs["L0_attn_basis"], dtype=np.float32)
    qc = np.asarray(inputs["L0_attn_q_coeffs"], dtype=np.float32)
    kc = np.asarray(inputs["L0_attn_k_coeffs"], dtype=np.float32)
    vcc = np.asarray(inputs["L0_attn_v_coeffs"], dtype=np.float32)
    oc = np.asarray(inputs["L0_attn_o_coeffs"], dtype=np.float32)
    dl = np.asarray(inputs["L0_attn_decay_logit"], dtype=np.float32)
    osc = np.asarray(inputs["L0_attn_out_scale"], dtype=np.float32)
    wred = np.asarray(inputs["L0_ffn_w_red"], dtype=np.float32)
    wg = np.asarray(inputs["L0_ffn_wg"], dtype=np.float32)
    w1 = np.asarray(inputs["L0_ffn_w1"], dtype=np.float32)
    b1 = np.asarray(inputs["L0_ffn_b1"], dtype=np.float32)
    w2 = np.asarray(inputs["L0_ffn_w2"], dtype=np.float32)
    b2 = np.asarray(inputs["L0_ffn_b2"], dtype=np.float32)

    if "nc" not in _CACHE:
        _CACHE["nc"] = build_nc()
    nc, _ = _CACHE["nc"]

    pos = np.arange(T)
    diff = pos[:, None] - pos[None, :]
    expn = np.where(diff > 0, np.maximum(diff - 1, 0), 1.0e30).astype(np.float32)
    amat = expn.astype(ml_dtypes.bfloat16)
    ident = np.eye(128, dtype=np.float32)

    shared = {
        "w1": np.ascontiguousarray(w1.reshape(NE * 1024, 4096)),
        "w2": np.ascontiguousarray(w2.reshape(NE * 4096, 1024)),
        "basisT": np.ascontiguousarray(basis.T),
        "qcT": np.ascontiguousarray(qc.T),
        "kcT": np.ascontiguousarray(kc.T),
        "vcT": np.ascontiguousarray(vcc.T),
        "ocT": np.ascontiguousarray(oc.T),
        "wred": np.ascontiguousarray(wred),
        "wg": np.ascontiguousarray(wg),
        "b1t": np.ascontiguousarray(
            b1.reshape(NE, 32, 128).transpose(2, 0, 1).reshape(128, 256)),
        "b2t": np.ascontiguousarray(
            b2.reshape(NE, 8, 128).transpose(2, 0, 1).reshape(128, 64)),
        "dlog": np.ascontiguousarray(np.broadcast_to(dl.reshape(1, 1), (128, 1))),
        "oscale": np.ascontiguousarray(np.broadcast_to(osc.reshape(1, 1), (128, 1))),
        "ident": ident,
        "amat": amat,
    }
    in_maps = []
    for c in range(NCORE):
        m = dict(shared)
        m["x"] = np.ascontiguousarray(x[2 * c:2 * c + 2].reshape(NSEQ * T, 1024))
        in_maps.append(m)

    res = run_bass_kernel_spmd(nc, in_maps, core_ids=list(range(NCORE)))
    out = np.concatenate(
        [r["out"].reshape(NSEQ, T, 1024) for r in res.results], axis=0)
    return out


if __name__ == "__main__":
    import time
    t0 = time.time()
    nc, n_inst = build_nc()
    print("build s:", round(time.time() - t0, 1))
    print("instructions:", n_inst, "total:", sum(n_inst.values()))
